# version 05 (final)
# baseline (speedup 1.0000x reference)
import numpy as np
from contextlib import ExitStack

import concourse.bass as bass
import concourse.mybir as mybir
from concourse.bass_utils import run_bass_kernel_spmd

F32 = mybir.dt.float32
AF = mybir.ActivationFunctionType
ALU = mybir.AluOpType
AX = mybir.AxisListType

B, S, D, H, HD = 4, 4096, 2048, 16, 128
T = 2048
NCH = 16
TWO_PI = 6.283185307179586
PI = 3.141592653589793
INV_2PI = 0.15915494309189535
EPS = 1e-5
ISQ = float(1.0 / np.sqrt(128.0))
ARENA = 47104
I32 = mybir.dt.int32


class Sem:
    def __init__(self, h, step):
        self.h, self.step, self.n = h, step, 0

    def inc(self, instr):
        self.n += self.step
        instr.then_inc(self.h, self.step)
        return self.n

    def dma(self, e, out, in_):
        e.wait_ge(self.h, self.n)
        return self.inc(e.dma_start(out=out, in_=in_))


def build(phases=4, debug=0):
    nc = bass.Bass("TRN2", target_bir_lowering=False)
    EI, IN = "ExternalInput", "Internal"
    xin = nc.dram_tensor("xin", [D, S], F32, kind=EI).ap()
    wq_a = nc.dram_tensor("wq_a", [16, 128, D], F32, kind=EI).ap()
    wk_a = nc.dram_tensor("wk_a", [16, 128, D], F32, kind=EI).ap()
    wv_a = nc.dram_tensor("wv_a", [16, 128, D], F32, kind=EI).ap()
    wo_a = nc.dram_tensor("wo_a", [16, 128, D], F32, kind=EI).ap()
    invf = nc.dram_tensor("invf", [1, 64], F32, kind=EI).ap()
    tvec = nc.dram_tensor("tvec", [1, S], F32, kind=EI).ap()
    eyein = nc.dram_tensor("eyein", [128, 128], F32, kind=EI).ap()
    wq_b = nc.dram_tensor("wq_b", [16, 128, D], F32, kind=EI).ap()
    wk_b = nc.dram_tensor("wk_b", [16, 128, D], F32, kind=EI).ap()
    wv_b = nc.dram_tensor("wv_b", [16, 128, D], F32, kind=EI).ap()
    wo_b = nc.dram_tensor("wo_b", [16, 128, D], F32, kind=EI).ap()
    bqkvo = nc.dram_tensor("bqkvo", [128, 64], F32, kind=EI).ap()
    wgu = nc.dram_tensor("wgu", [4, 32, 128, D], F32, kind=EI).ap()
    wdn = nc.dram_tensor("wdn", [4, 16, 128, D], F32, kind=EI).ap()
    out = nc.dram_tensor("out", [D, T], F32, kind="ExternalOutput").ap()

    q_s = nc.dram_tensor("q_s", [128, 16, 16, 128], F32, kind=IN).ap()
    k_s = nc.dram_tensor("k_s", [128, 16, 16, 128], F32, kind=IN).ap()
    v_s = nc.dram_tensor("v_s", [128, 16, 16, 128], F32, kind=IN).ap()
    x1_s = nc.dram_tensor("x1_s", [D, T], F32, kind=IN).ap()
    cc_i = nc.dram_tensor("cc_i", [128, 2064], F32, kind=IN).ap()
    cc_o = nc.dram_tensor("cc_o", [128, 2064], F32, kind=IN).ap()

    dbg = None
    if debug:
        dbg = nc.dram_tensor("dbg", [D, S], F32, kind="ExternalOutput").ap()

    es = ExitStack()
    AR = es.enter_context(nc.sbuf_tensor("AR", [128, ARENA], F32))
    ST = es.enter_context(nc.sbuf_tensor("STt", [128, 2048], F32))
    ST2 = es.enter_context(nc.sbuf_tensor("ST2t", [128, 2048], F32))
    SM = es.enter_context(nc.sbuf_tensor("SMt", [128, 640], F32))
    RW = es.enter_context(nc.sbuf_tensor("RWt", [128, 960], F32))
    PS = [es.enter_context(nc.psum_tensor("ps%d" % i, [128, 512], F32))
          for i in range(8)]
    sW = Sem(es.enter_context(nc.semaphore("sW")), 16)
    sG = Sem(es.enter_context(nc.semaphore("sG")), 16)
    sT = Sem(es.enter_context(nc.semaphore("sT")), 1)
    sS = Sem(es.enter_context(nc.semaphore("sS")), 1)
    sV = Sem(es.enter_context(nc.semaphore("sV")), 1)
    sC = Sem(es.enter_context(nc.semaphore("sC")), 1)

    rM = ST[0:1, :]
    rVAR = ST[32:33, :]
    rIST = ST[64:65, :]
    rTMP = ST2[32:33, :]
    ones = SM[:, 0:512]
    eye = SM[:, 512:640]
    pib = RW[0:64, 944:945]
    RW_MAXQ, RW_MAXK, RW_MHS, RW_NRM = 0, 16, 32, 48
    RW_NEGM, RW_BIAS, RW_INVF, RW_KSC = 512, 528, 592, 656
    rRD = RW[32:33, 0:512]
    negm = lambda hh: RW[:, RW_NEGM + hh:RW_NEGM + hh + 1]
    bias_b = lambda w, hh: RW[:, RW_BIAS + w * 16 + hh:RW_BIAS + w * 16 + hh + 1]

    X = AR[:, 0:32768]
    xch = lambda i: AR[:, i * 2048:(i + 1) * 2048]
    TRS = AR[:, 32768:34816]
    TRC = AR[:, 34816:36864]
    WS = [AR[:, 36864:38912], AR[:, 38912:40960]]
    SQ = AR[:, 40960:43008]
    STG = AR[:, 43008:45056]
    BCt = AR[:, 45056:47104]
    KH = AR[:, 32768:36864]
    VT = AR[:, 36864:40960]
    QH = AR[:, 40960:43008]
    EXPT = [AR[:, 43008:43520], AR[:, 43520:44032]]
    DT = AR[:, 44032:44544]
    VSTG = [AR[:, 44544:45056], AR[:, 45056:45568]]
    RDB = AR[:, 45568:46080]
    WS5 = [AR[:, 32768:34816], AR[:, 34816:36864]]
    XST = AR[:, 36864:38912]
    X1ST = AR[:, 38912:40960]
    XH = AR[:, 32768:36864]
    xhch = lambda i: AR[:, 32768 + i * 256:32768 + (i + 1) * 256]
    ITR = AR[:, 36864:40960]
    itch = lambda i: AR[:, 36864 + i * 256:36864 + (i + 1) * 256]
    WS6 = [AR[:, 40960:41984], AR[:, 41984:43008]]
    MBE = AR[:, 43008:43264]
    IBE = AR[:, 43264:43520]
    TMPE = AR[:, 43520:43776]
    BC6 = AR[:, 43776:45824]
    KF = AR[:, 32768:34816]
    VF = AR[:, 34816:36864]
    KTL = AR[:, 36864:37120]
    KVB = AR[:, 37120:39168]
    WS7 = AR[:, 39168:41216]
    ELT = AR[:, 41216:43264]
    CB7 = AR[:, 32768:36864]
    cbch = lambda i: AR[:, 32768 + i * 256:32768 + (i + 1) * 256]
    QE7 = AR[:, 41216:41472]
    RD7 = AR[:, 41472:41728]
    M3B = AR[:, 41728:41984]
    QTMP = AR[:, 41984:42240]

    blk = nc.Block()
    block = blk.__enter__()

    def barrier():
        vals = []
        @block.tensor
        def _(e):
            e.wait_ge(sT.h, sT.n)
            sT.n += 1
            e.sem_inc(sT.h, 1)
        @block.scalar
        def _(e):
            e.wait_ge(sS.h, sS.n)
            sS.n += 1
            e.sem_inc(sS.h, 1)
        @block.vector
        def _(e):
            e.wait_ge(sV.h, sV.n)
            sV.n += 1
            e.sem_inc(sV.h, 1)
        @block.gpsimd
        def _(e):
            e.wait_ge(sC.h, sC.n)
            sC.n += 1
            e.sem_inc(sC.h, 1)
        vals = [(sT, sT.n), (sS, sS.n), (sV, sV.n), (sC, sC.n),
                (sW, sW.n), (sG, sG.n)]
        for sec in ("tensor", "scalar", "vector", "gpsimd", "sync"):
            @getattr(block, sec)
            def _(e):
                for s_, v_ in vals:
                    if v_ > 0:
                        e.wait_ge(s_.h, v_)

    @block.sync
    def _(e):
        e.wait_ge(sW.h, sW.n)
        sW.inc(e.dma_start(out=RW[0:1, RW_INVF:RW_INVF + 64], in_=invf[:, :]))
        e.wait_ge(sW.h, sW.n)
        sW.inc(e.dma_start(out=SM[:, 512:640], in_=eyein[:, :]))
        e.wait_ge(sW.h, sW.n)
        sW.inc(e.dma_start(out=RW[:, RW_BIAS:RW_BIAS + 64], in_=bqkvo[:, :]))
        e.wait_ge(sW.h, sW.n)
        sW.inc(e.dma_start(out=ST2[0:1, 0:2048], in_=tvec[:, 0:2048]))
        e.wait_ge(sW.h, sW.n)
        i = e.dma_start(out=ST2[64:65, 0:2048], in_=tvec[:, 2048:4096])
        sW.inc(i)

    @block.vector
    def _(e):
        e.memset(ones[:, :], 1.0)
        e.memset(RW[0:1, 0:64], 0.0)
        i = e.memset(pib, 0.0)
        v = sV.inc(i)
        e.wait_ge(sV.h, v)
        i = e.tensor_scalar_add(pib, pib, PI / 2)
        sV.inc(i)

    @block.scalar
    def _(e):
        e.wait_ge(sW.h, sW.n)
        i = e.activation(RW[64:65, RW_INVF:RW_INVF + 64],
                         RW[0:1, RW_INVF:RW_INVF + 64], AF.Copy)
        sS.inc(i)
    barrier()

    def build_trig(half):
        tv = ST2[0:1, :] if half == 0 else ST2[64:65, :]
        ifr = (RW[0:1, RW_INVF:RW_INVF + 64] if half == 0
               else RW[64:65, RW_INVF:RW_INVF + 64])
        amm = []
        @block.tensor
        def _(e):
            for j in range(4):
                i = e.matmul(PS[4 + j][0:64, :], ifr,
                             tv[:, j * 512:(j + 1) * 512],
                             start=True, stop=True)
                amm.append(sT.inc(i))
        vr = []
        @block.vector
        def _(e):
            for j in range(4):
                e.wait_ge(sT.h, amm[j])
                ps = PS[4 + j][0:64, :]
                cl = slice(j * 512, (j + 1) * 512)
                y = SQ[0:64, cl]
                s_ = STG[0:64, cl]
                steps = [
                    lambda: e.tensor_scalar_mul(y, ps, INV_2PI),
                    lambda: e.tensor_copy(s_.bitcast(I32), y),
                    lambda: e.tensor_copy(y, s_.bitcast(I32)),
                    lambda: e.scalar_tensor_tensor(s_, y, -TWO_PI, ps,
                                                   ALU.mult, ALU.add),
                    lambda: e.tensor_single_scalar(y, s_, PI, ALU.is_gt),
                    lambda: e.scalar_tensor_tensor(s_, y, -TWO_PI, s_,
                                                   ALU.mult, ALU.add),
                    lambda: e.tensor_single_scalar(y, s_, -PI, ALU.is_lt),
                    lambda: e.scalar_tensor_tensor(s_, y, TWO_PI, s_,
                                                   ALU.mult, ALU.add),
                ]
                for k, fn in enumerate(steps):
                    v = sV.inc(fn())
                    if k < len(steps) - 1:
                        e.wait_ge(sV.h, v)
                vr.append(sV.n)
        strig = []
        @block.scalar
        def _(e):
            for j in range(4):
                e.wait_ge(sV.h, vr[j])
                cl = slice(j * 512, (j + 1) * 512)
                s_ = STG[0:64, cl]
                e.activation(TRS[0:64, cl], s_, AF.Sin)
                i = e.activation(SQ[64:128, cl], s_, AF.Abs)
                v = sS.inc(i)
                e.wait_ge(sS.h, v)
                i = e.activation(TRC[0:64, cl], SQ[64:128, cl], AF.Sin,
                                 scale=-1.0, bias=pib)
                strig.append(sS.inc(i))
        @block.gpsimd
        def _(e):
            e.wait_ge(sS.h, strig[-1])
            e.wait_ge(sG.h, sG.n)
            sG.inc(e.dma_start(out=TRS[64:128, :], in_=TRS[0:64, :]))
            e.wait_ge(sG.h, sG.n)
            i = e.dma_start(out=TRC[64:128, :], in_=TRC[0:64, :])
            sG.inc(i)
        return strig[-1]

    build_trig(0)
    barrier()

    def load_X(src, cols):
        vals = []
        @block.sync
        def _(e):
            for ch in range(NCH):
                e.wait_ge(sW.h, sW.n)
                i = e.dma_start(out=xch(ch),
                                in_=src[ch * 128:(ch + 1) * 128, cols])
                vals.append(sW.inc(i))
        return vals

    def stats_center(x_ready_vals, bc_dst, center=True):
        t_sum = []
        @block.tensor
        def _(e):
            for ch in range(NCH):
                if x_ready_vals is not None:
                    e.wait_ge(sW.h, x_ready_vals[ch])
                for s4 in range(4):
                    i = e.matmul(PS[4 + s4][0:1, :], ones[:, 0:1],
                                 xch(ch)[:, s4 * 512:(s4 + 1) * 512],
                                 start=(ch == 0), stop=(ch == NCH - 1))
                    if ch == NCH - 1:
                        t_sum.append(sT.inc(i))
        v_sum = []
        @block.vector
        def _(e):
            e.wait_ge(sT.h, t_sum[-1])
            for s4 in range(4):
                i = e.tensor_copy(rM[:, s4 * 512:(s4 + 1) * 512],
                                  PS[4 + s4][0:1, :])
                v_sum.append(sV.inc(i))
        t_sum_last = t_sum[-1]
        s_sq = []
        @block.scalar
        def _(e):
            for ch in range(NCH):
                if x_ready_vals is not None:
                    e.wait_ge(sW.h, x_ready_vals[ch])
                if ch >= 1:
                    e.wait_ge(sT.h, t_sum_last + ch)
                i = e.activation(SQ[:, :], xch(ch), AF.Square)
                s_sq.append(sS.inc(i))
        t_sq = []
        @block.tensor
        def _(e):
            e.wait_ge(sV.h, v_sum[-1])
            for ch in range(NCH):
                e.wait_ge(sS.h, s_sq[ch])
                for s4 in range(4):
                    i = e.matmul(PS[4 + s4][0:1, :], ones[:, 0:1],
                                 SQ[:, s4 * 512:(s4 + 1) * 512],
                                 start=(ch == 0), stop=(ch == NCH - 1))
                    if s4 == 3:
                        t_sq.append(sT.inc(i))
        assert t_sq[0] == t_sum_last + 1
        st1 = []
        @block.vector
        def _(e):
            e.wait_ge(sT.h, t_sq[-1])
            for s4 in range(4):
                e.tensor_copy(rVAR[:, s4 * 512:(s4 + 1) * 512],
                              PS[4 + s4][0:1, :])
            i = e.tensor_scalar_mul(rM, rM, 1.0 / D)
            v = sV.inc(i)
            e.wait_ge(sV.h, v)
            i = e.tensor_mul(rTMP, rM, rM)
            v = sV.inc(i)
            e.wait_ge(sV.h, v)
            i = e.scalar_tensor_tensor(rVAR, rVAR, 1.0 / D, rTMP,
                                       ALU.mult, ALU.subtract)
            v = sV.inc(i)
            e.wait_ge(sV.h, v)
            i = e.tensor_scalar_add(rVAR, rVAR, EPS)
            st1.append(sV.inc(i))
        st2 = []
        @block.scalar
        def _(e):
            e.wait_ge(sV.h, st1[0])
            i = e.activation(rVAR, rVAR, AF.Sqrt)
            st2.append(sS.inc(i))
        st3 = []
        @block.vector
        def _(e):
            e.wait_ge(sS.h, st2[0])
            i = e.reciprocal(rIST, rVAR)
            st3.append(sV.inc(i))
        if not center:
            return
        bc_m = []
        @block.tensor
        def _(e):
            e.wait_ge(sV.h, st1[0])
            for s4 in range(4):
                i = e.matmul(PS[4 + s4][:, :], ones[0:1, 0:128],
                             rM[:, s4 * 512:(s4 + 1) * 512],
                             start=True, stop=True)
                bc_m.append(sT.inc(i))
        cen = []
        @block.vector
        def _(e):
            for s4 in range(4):
                e.wait_ge(sT.h, bc_m[s4])
                i = e.tensor_copy(bc_dst[:, s4 * 512:(s4 + 1) * 512],
                                  PS[4 + s4][:, :])
            v = sV.inc(i)
            e.wait_ge(sV.h, v)
            for ch in range(NCH):
                i = e.tensor_sub(xch(ch), xch(ch), bc_dst[:, :])
                cen.append(sV.inc(i))
        bc_i = []
        @block.tensor
        def _(e):
            e.wait_ge(sV.h, cen[-1])
            e.wait_ge(sV.h, st3[0])
            for s4 in range(4):
                i = e.matmul(PS[4 + s4][:, :], ones[64:65, 0:128],
                             rIST[:, s4 * 512:(s4 + 1) * 512],
                             start=True, stop=True)
                bc_i.append(sT.inc(i))
        @block.vector
        def _(e):
            for s4 in range(4):
                e.wait_ge(sT.h, bc_i[s4])
                i = e.tensor_copy(bc_dst[:, s4 * 512:(s4 + 1) * 512],
                                  PS[4 + s4][:, :])
            sV.inc(i)

    def proj_a(wt, dst, dst_cols, rope, nrm_col):
        rec_t = {}
        rec_ev = {}
        rec_sq = {}
        rec_nm = {}
        rec_vn = {}
        rec_dm = {}
        rec_ws = {}
        for oc in range(16):
            bs = 0 if oc % 2 == 0 else 4
            @block.sync
            def _(e, oc=oc):
                if oc >= 2:
                    e.wait_ge(sT.h, rec_t[oc - 2])
                e.wait_ge(sW.h, sW.n)
                i = e.dma_start(out=WS[oc % 2][:, :], in_=wt[oc])
                rec_ws[oc] = sW.inc(i)
            @block.tensor
            def _(e, oc=oc, bs=bs):
                e.wait_ge(sW.h, rec_ws[oc])
                if oc >= 2:
                    e.wait_ge(sV.h, rec_vn[oc - 2] if nrm_col is not None
                              else rec_ev[oc - 2])
                for ch in range(NCH):
                    for s4 in range(4):
                        i = e.matmul(PS[bs + s4][:, :],
                                     WS[oc % 2][:, ch * 128:(ch + 1) * 128],
                                     xch(ch)[:, s4 * 512:(s4 + 1) * 512],
                                     start=(ch == 0), stop=(ch == NCH - 1))
                        if ch == NCH - 1:
                            rec_t[(oc, s4)] = sT.inc(i)
                rec_t[oc] = sT.n
            @block.vector
            def _(e, oc=oc, bs=bs):
                if oc >= 1:
                    e.wait_ge(sG.h, rec_dm[oc - 1])
                    if nrm_col is not None:
                        e.wait_ge(sS.h, rec_sq[oc - 1])
                        e.wait_ge(sT.h, rec_nm[oc - 1])
                for s4 in range(4):
                    e.wait_ge(sT.h, rec_t[(oc, s4)])
                    i = e.tensor_tensor(STG[:, s4 * 512:(s4 + 1) * 512],
                                        PS[bs + s4][:, :],
                                        BCt[:, s4 * 512:(s4 + 1) * 512],
                                        ALU.mult)
                if rope:
                    v = sV.inc(i)
                    e.wait_ge(sV.h, v)
                    x1, x2 = STG[0:64, :], STG[64:128, :]
                    steps = [
                        lambda: e.tensor_mul(SQ[0:64, :], x2, TRS[64:128, :]),
                        lambda: e.tensor_mul(SQ[64:128, :], x1, TRS[0:64, :]),
                        lambda: e.tensor_mul(x1, x1, TRC[0:64, :]),
                        lambda: e.tensor_sub(x1, x1, SQ[0:64, :]),
                        lambda: e.tensor_mul(x2, x2, TRC[64:128, :]),
                        lambda: e.tensor_add(x2, x2, SQ[64:128, :]),
                    ]
                    for k, fn in enumerate(steps):
                        i = fn()
                        if k < len(steps) - 1:
                            v = sV.inc(i)
                            e.wait_ge(sV.h, v)
                rec_ev[oc] = sV.inc(i)
            if nrm_col is not None:
                @block.scalar
                def _(e, oc=oc):
                    e.wait_ge(sV.h, rec_ev[oc])
                    i = e.activation(SQ[:, :], STG[:, :], AF.Square)
                    rec_sq[oc] = sS.inc(i)
                @block.tensor
                def _(e, oc=oc, bs=bs):
                    e.wait_ge(sS.h, rec_sq[oc])
                    for s4 in range(4):
                        i = e.matmul(PS[bs + s4][0:1, :], ones[:, 0:1],
                                     SQ[:, s4 * 512:(s4 + 1) * 512],
                                     start=True, stop=True)
                        rec_nm[(oc, s4)] = sT.inc(i)
                    rec_nm[oc] = sT.n
                @block.vector
                def _(e, oc=oc, bs=bs):
                    e.wait_ge(sT.h, rec_nm[oc])
                    for s4 in range(4):
                        i = e.tensor_reduce(
                            RW[0:1, RW_NRM + s4:RW_NRM + s4 + 1],
                            PS[bs + s4][0:1, :], AX.X, ALU.max)
                    v = sV.inc(i)
                    e.wait_ge(sV.h, v)
                    i = e.tensor_reduce(RW[0:1, RW_MHS:RW_MHS + 1],
                                        RW[0:1, RW_NRM:RW_NRM + 4], AX.X,
                                        ALU.max)
                    v = sV.inc(i)
                    e.wait_ge(sV.h, v)
                    i = e.tensor_tensor(
                        RW[0:1, nrm_col + oc:nrm_col + oc + 1],
                        RW[0:1, nrm_col + oc:nrm_col + oc + 1],
                        RW[0:1, RW_MHS:RW_MHS + 1], ALU.max)
                    rec_vn[oc] = sV.inc(i)
            @block.gpsimd
            def _(e, oc=oc):
                e.wait_ge(sV.h, rec_ev[oc])
                e.wait_ge(sG.h, sG.n)
                i = e.dma_start(out=dst[:, :, oc, :], in_=STG[:, :])
                rec_dm[oc] = sG.inc(i)

    xv = load_X(xin, slice(0, T))
    stats_center(xv, BCt)
    barrier()
    if debug == 1:
        _dump = [(dbg[c * 128:(c + 1) * 128, 0:2048], xch(c))
                 for c in range(16)]
        _dump += [(dbg[0:128, 2048:4096], BCt[:, :]),
                  (dbg[128:256, 2048:4096], ST[:, :]),
                  (dbg[256:384, 2048:4096], TRS[:, :]),
                  (dbg[384:512, 2048:4096], TRC[:, :])]
        @block.gpsimd
        def _(e):
            for d_, s_ in _dump:
                e.wait_ge(sG.h, sG.n)
                i = e.dma_start(out=d_, in_=s_)
                sG.inc(i)
            e.wait_ge(sG.h, sG.n)
        blk.__exit__(None, None, None)
        es.close()
        return nc

    proj_a(wq_a, q_s, slice(0, T), True, None)
    barrier()
    proj_a(wk_a, k_s, slice(0, T), True, None)
    barrier()
    proj_a(wv_a, v_s, slice(0, T), False, None)
    barrier()

    QF = AR[:, 32768:34816]
    KF2 = AR[:, 34816:36864]
    VF2 = AR[:, 36864:38912]
    Qt = AR[:, 38912:40960]
    Kt = AR[:, 40960:43008]
    Vt = AR[:, 43008:45056]
    Ot = AR[:, 45056:47104]
    SC = RW[:, 700:716]
    EW = RW[:, 716:732]
    Mc = RW[:, 740:741]
    Dc = RW[:, 741:742]
    rec4 = {}
    for tl in range(16):
        tc = slice(tl * 128, (tl + 1) * 128)
        @block.sync
        def _(e, tl=tl):
            if tl >= 1:
                e.wait_ge(sT.h, rec4[(tl - 1, "trin")])
            rec4[(tl, "ldq")] = sW.dma(e, out=QF[:, :], in_=q_s[:, tl, :, :])
            rec4[(tl, "ldk")] = sW.dma(e, out=KF2[:, :], in_=k_s[:, tl, :, :])
            rec4[(tl, "ldv")] = sW.dma(e, out=VF2[:, :], in_=v_s[:, tl, :, :])
        for j in range(48):
            srcb = (QF, KF2, VF2)[j // 16]
            dstb = (Qt, Kt, Vt)[j // 16]
            ldk = ("ldq", "ldk", "ldv")[j // 16]
            ch = j % 16
            @block.tensor
            def _(e, tl=tl, j=j, srcb=srcb, ldk=ldk, ch=ch):
                if j == 0 and tl >= 1:
                    e.wait_ge(sV.h, rec4[(tl - 1, "mixdone")])
                if ch == 0:
                    e.wait_ge(sW.h, rec4[(tl, ldk)])
                if j >= 4:
                    e.wait_ge(sV.h, rec4[(tl, "ev", j - 4)])
                ii = e.transpose(PS[4 + j % 4][:, 0:128],
                                 srcb[:, ch * 128:(ch + 1) * 128],
                                 eye[:, :])
                rec4[(tl, "tr", j)] = sT.inc(ii)
                if j == 47:
                    rec4[(tl, "trin")] = sT.n
            @block.vector
            def _(e, tl=tl, j=j, dstb=dstb, ch=ch):
                e.wait_ge(sT.h, rec4[(tl, "tr", j)])
                ii = e.tensor_copy(dstb[:, ch * 128:(ch + 1) * 128],
                                   PS[4 + j % 4][:, 0:128])
                rec4[(tl, "ev", j)] = sV.inc(ii)
        for h in range(16):
            @block.vector
            def _(e, tl=tl, h=h):
                e.wait_ge(sV.h, sV.n)
                for t in range(16):
                    ii = e.tensor_mul(ST2[:, t * 128:(t + 1) * 128],
                                      Qt[:, h * 128:(h + 1) * 128],
                                      Kt[:, t * 128:(t + 1) * 128])
                v_ = sV.inc(ii)
                e.wait_ge(sV.h, v_)
                for t in range(16):
                    ii = e.tensor_reduce(SC[:, t:t + 1],
                                         ST2[:, t * 128:(t + 1) * 128],
                                         AX.X, ALU.add)
                v_ = sV.inc(ii)
                e.wait_ge(sV.h, v_)
                ii = e.tensor_reduce(Mc, SC[:, 0:16], AX.X, ALU.max)
                v_ = sV.inc(ii)
                e.wait_ge(sV.h, v_)
                ii = e.tensor_scalar(SC[:, 0:16], SC[:, 0:16], Mc, None,
                                     ALU.subtract)
                rec4[(tl, h, "sc")] = sV.inc(ii)
            @block.scalar
            def _(e, tl=tl, h=h):
                e.wait_ge(sV.h, rec4[(tl, h, "sc")])
                ii = e.activation(EW[:, 0:16], SC[:, 0:16], AF.Exp, scale=ISQ)
                rec4[(tl, h, "exp")] = sS.inc(ii)
            @block.vector
            def _(e, tl=tl, h=h):
                e.wait_ge(sS.h, rec4[(tl, h, "exp")])
                ii = e.tensor_reduce(Dc, EW[:, 0:16], AX.X, ALU.add)
                v_ = sV.inc(ii)
                e.wait_ge(sV.h, v_)
                ii = e.reciprocal(Dc, Dc)
                v_ = sV.inc(ii)
                e.wait_ge(sV.h, v_)
                ii = e.tensor_scalar(EW[:, 0:16], EW[:, 0:16], Dc, None,
                                     ALU.mult)
                v_ = sV.inc(ii)
                e.wait_ge(sV.h, v_)
                for t in range(16):
                    ii = e.tensor_scalar(ST2[:, t * 128:(t + 1) * 128],
                                         Vt[:, t * 128:(t + 1) * 128],
                                         EW[:, t:t + 1], None, ALU.mult)
                v_ = sV.inc(ii)
                e.wait_ge(sV.h, v_)
                for stride in (1, 2, 4):
                    for t0 in range(0, 16, 2 * stride):
                        ii = e.tensor_add(
                            ST2[:, t0 * 128:(t0 + 1) * 128],
                            ST2[:, t0 * 128:(t0 + 1) * 128],
                            ST2[:, (t0 + stride) * 128:(t0 + stride + 1) * 128])
                    v_ = sV.inc(ii)
                    e.wait_ge(sV.h, v_)
                ii = e.tensor_add(Ot[:, h * 128:(h + 1) * 128],
                                  ST2[:, 0:128], ST2[:, 8 * 128:9 * 128])
                rec4[(tl, h, "mix")] = sV.inc(ii)
        for h in range(16):
            @block.tensor
            def _(e, tl=tl, h=h):
                e.wait_ge(sV.h, rec4[(tl, h, "mix")])
                if h >= 4:
                    e.wait_ge(sV.h, rec4[(tl, "oev", h - 4)])
                ii = e.transpose(PS[4 + h % 4][:, 0:128],
                                 Ot[:, h * 128:(h + 1) * 128], eye[:, :])
                rec4[(tl, "otr", h)] = sT.inc(ii)
            @block.vector
            def _(e, tl=tl, tc=tc, h=h):
                e.wait_ge(sT.h, rec4[(tl, "otr", h)])
                ii = e.tensor_copy(xch(h)[:, tc], PS[4 + h % 4][:, 0:128])
                rec4[(tl, "oev", h)] = sV.inc(ii)
                if h == 15:
                    rec4[(tl, "mixdone")] = sV.n
    barrier()

    rec_t5 = {}
    rec_ev5 = {}
    rec_dm5 = {-1: sG.n}
    for oc in range(16):
        bs = 0 if oc % 2 == 0 else 4
        @block.sync
        def _(e, oc=oc):
            if oc >= 2:
                e.wait_ge(sT.h, rec_t5[oc - 2])
            e.wait_ge(sW.h, sW.n)
            i = e.dma_start(out=WS5[oc % 2][:, :], in_=wo_a[oc])
            rec_t5[("w", oc)] = sW.inc(i)
            if oc >= 1:
                e.wait_ge(sV.h, rec_ev5[(oc - 1, 3)])
            e.wait_ge(sW.h, sW.n)
            i = e.dma_start(out=XST[:, :],
                            in_=xin[oc * 128:(oc + 1) * 128, 0:T])
            rec_t5[("x", oc)] = sW.inc(i)
        @block.tensor
        def _(e, oc=oc, bs=bs):
            e.wait_ge(sW.h, rec_t5[("w", oc)])
            if oc >= 2:
                e.wait_ge(sV.h, rec_ev5[(oc - 2, 3)])
            for ch in range(NCH):
                for s4 in range(4):
                    i = e.matmul(PS[bs + s4][:, :],
                                 WS5[oc % 2][:, ch * 128:(ch + 1) * 128],
                                 xch(ch)[:, s4 * 512:(s4 + 1) * 512],
                                 start=(ch == 0), stop=(ch == NCH - 1))
                    if ch == NCH - 1:
                        rec_t5[(oc, s4)] = sT.inc(i)
            rec_t5[oc] = sT.n
        @block.vector
        def _(e, oc=oc, bs=bs):
            e.wait_ge(sW.h, rec_t5[("x", oc)])
            if oc >= 1:
                e.wait_ge(sG.h, rec_dm5[oc - 1])
            for s4 in range(4):
                e.wait_ge(sT.h, rec_t5[(oc, s4)])
                i = e.tensor_tensor(X1ST[:, s4 * 512:(s4 + 1) * 512],
                                    PS[bs + s4][:, :],
                                    XST[:, s4 * 512:(s4 + 1) * 512], ALU.add)
                rec_ev5[(oc, s4)] = sV.inc(i)
        @block.gpsimd
        def _(e, oc=oc):
            e.wait_ge(sV.h, rec_ev5[(oc, 3)])
            e.wait_ge(sG.h, sG.n)
            i = e.dma_start(out=x1_s[oc * 128:(oc + 1) * 128, :],
                            in_=X1ST[:, :])
            rec_dm5[oc] = sG.inc(i)
            if phases == 1:
                e.wait_ge(sG.h, sG.n)
                i = e.dma_start(out=out[oc * 128:(oc + 1) * 128, :],
                                in_=X1ST[:, :])
                rec_dm5[oc] = sG.inc(i)
    barrier()
    if phases == 1:
        @block.gpsimd
        def _(e):
            e.wait_ge(sG.h, sG.n)
        blk.__exit__(None, None, None)
        es.close()
        return nc

    WSA = AR[:, 40960:43008]
    WSB = AR[:, 43776:45824]

    def ffn_phase():
        rec_ev = {}
        for ei in range(4):
            for hf in range(2):
                it = ei * 2 + hf
                off = ei + 1024 * hf
                tsl = slice(off, off + 1021, 4)
                bce = []
                @block.tensor
                def _(e, it=it, tsl=tsl):
                    if (it - 1, "last") in rec_ev:
                        e.wait_ge(sV.h, rec_ev[(it - 1, "last")])
                    i = e.matmul(PS[6][:, 0:256], ones[0:1, 0:128],
                                 rM[:, tsl], start=True, stop=True)
                    bce.append(sT.inc(i))
                    i = e.matmul(PS[7][:, 0:256], ones[64:65, 0:128],
                                 rIST[:, tsl], start=True, stop=True)
                    bce.append(sT.inc(i))
                vgat = []
                @block.vector
                def _(e, it=it, tsl=tsl):
                    e.wait_ge(sT.h, bce[1])
                    e.tensor_copy(MBE[:, :], PS[6][:, 0:256])
                    i = e.tensor_copy(IBE[:, :], PS[7][:, 0:256])
                    v = sV.inc(i)
                    e.wait_ge(sV.h, v)
                    for ch in range(NCH):
                        i = e.tensor_sub(xhch(ch), xch(ch)[:, tsl],
                                         MBE[:, :])
                        v = sV.inc(i)
                        e.wait_ge(sV.h, v)
                        i = e.tensor_mul(xhch(ch), xhch(ch), IBE[:, :])
                        sV.inc(i)
                    vgat.append(sV.n)
                rec_w, rec_tg, rec_sl, rec_iv = {}, {}, {}, {}
                for jj in range(16):
                    for up in range(2):
                        oc = jj + 16 * up
                        slot = WSA if up == 0 else WSB
                        @block.sync
                        def _(e, it=it, ei=ei, oc=oc, jj=jj, up=up, slot=slot):
                            if jj >= 1:
                                e.wait_ge(sT.h, rec_tg[(jj - 1) + 16 * up])
                            else:
                                e.wait_ge(sV.h, vgat[0])
                            e.wait_ge(sW.h, sW.n)
                            i = e.dma_start(out=slot[:, :], in_=wgu[ei, oc])
                            rec_w[oc] = sW.inc(i)
                        @block.tensor
                        def _(e, it=it, oc=oc, jj=jj, up=up, slot=slot):
                            e.wait_ge(sW.h, rec_w[oc])
                            e.wait_ge(sV.h, vgat[0])
                            if jj >= 1:
                                if up == 0:
                                    e.wait_ge(sS.h, rec_sl[jj - 1])
                                else:
                                    e.wait_ge(sV.h, rec_iv[(jj - 1) + 16])
                            for ch in range(NCH):
                                i = e.matmul(PS[up][:, 0:256],
                                             slot[:, ch * 128:(ch + 1) * 128],
                                             xhch(ch), start=(ch == 0),
                                             stop=(ch == NCH - 1))
                            rec_tg[oc] = sT.inc(i)
                        if up == 0:
                            @block.scalar
                            def _(e, oc=oc, jj=jj):
                                e.wait_ge(sT.h, rec_tg[oc])
                                if jj >= 1:
                                    e.wait_ge(sV.h, rec_iv[(jj - 1) + 16])
                                i = e.activation(TMPE[:, :], PS[0][:, 0:256],
                                                 AF.Silu)
                                rec_sl[jj] = sS.inc(i)
                        else:
                            @block.vector
                            def _(e, oc=oc, jj=jj):
                                e.wait_ge(sT.h, rec_tg[oc])
                                e.wait_ge(sS.h, rec_sl[jj])
                                i = e.tensor_tensor(itch(jj), PS[1][:, 0:256],
                                                    TMPE[:, :], ALU.mult)
                                rec_iv[oc] = sV.inc(i)
                rec_wd, rec_td = {}, {}
                for oc in range(16):
                    par = 2 + oc % 2
                    slot = WSA if oc % 2 == 0 else WSB
                    @block.sync
                    def _(e, it=it, ei=ei, oc=oc, slot=slot):
                        if oc >= 2:
                            e.wait_ge(sT.h, rec_td[oc - 2])
                        else:
                            e.wait_ge(sT.h, rec_tg[15 + 16 * (oc % 2)])
                        e.wait_ge(sW.h, sW.n)
                        i = e.dma_start(out=slot[:, :], in_=wdn[ei, oc])
                        rec_wd[oc] = sW.inc(i)
                    @block.tensor
                    def _(e, it=it, oc=oc, par=par, slot=slot):
                        e.wait_ge(sW.h, rec_wd[oc])
                        e.wait_ge(sV.h, rec_iv[31])
                        if oc >= 2:
                            e.wait_ge(sV.h, rec_ev[(it, oc - 2)])
                        for ch in range(NCH):
                            i = e.matmul(PS[par][:, 0:256],
                                         slot[:, ch * 128:(ch + 1) * 128],
                                         itch(ch), start=(ch == 0),
                                         stop=(ch == NCH - 1))
                        rec_td[oc] = sT.inc(i)
                    @block.vector
                    def _(e, it=it, oc=oc, par=par, tsl=tsl):
                        e.wait_ge(sT.h, rec_td[oc])
                        xb = xch(oc)[:, tsl]
                        i = e.tensor_tensor(xb, PS[par][:, 0:256], xb,
                                            ALU.add)
                        rec_ev[(it, oc)] = sV.inc(i)
                        if oc == 15:
                            rec_ev[(it, "last")] = sV.n

    xv = load_X(x1_s, slice(0, T))
    stats_center(xv, BC6, center=False)
    barrier()
    ffn_phase()
    barrier()
    if phases == 2:
        @block.gpsimd
        def _(e):
            for c in range(16):
                e.wait_ge(sG.h, sG.n)
                i = e.dma_start(out=out[c * 128:(c + 1) * 128, :], in_=xch(c))
                sG.inc(i)
            e.wait_ge(sG.h, sG.n)
        blk.__exit__(None, None, None)
        es.close()
        return nc

    stats_center(None, BC6)
    barrier()
    rec7 = {"t": {-1: sT.n}}
    kv_ev = {}
    for hh in range(H):
        rec_w = {}
        @block.sync
        def _(e, hh=hh):
            e.wait_ge(sT.h, rec7["t"][hh - 1])
            e.wait_ge(sW.h, sW.n)
            i = e.dma_start(out=WS7[:, :], in_=wk_b[hh])
            rec_w["k"] = sW.inc(i)
        rec_pj = {}
        @block.tensor
        def _(e, hh=hh):
            e.wait_ge(sW.h, rec_w["k"])
            if hh >= 1:
                e.wait_ge(sV.h, kv_ev[hh - 1])
            for ch in range(NCH):
                for s4 in range(4):
                    i = e.matmul(PS[s4][:, :],
                                 WS7[:, ch * 128:(ch + 1) * 128],
                                 xch(ch)[:, s4 * 512:(s4 + 1) * 512],
                                 start=(ch == 0), stop=(ch == NCH - 1))
                    if ch == NCH - 1:
                        rec_pj[("k", s4)] = sT.inc(i)
        rec_el = {}
        @block.vector
        def _(e, hh=hh):
            for s4 in range(4):
                e.wait_ge(sT.h, rec_pj[("k", s4)])
                i = e.tensor_tensor(KF[:, s4 * 512:(s4 + 1) * 512],
                                    PS[s4][:, :],
                                    BC6[:, s4 * 512:(s4 + 1) * 512], ALU.mult)
            v = sV.inc(i)
            e.wait_ge(sV.h, v)
            i = e.tensor_scalar_add(KF[:, :], KF[:, :], bias_b(1, hh))
            rec_el["kev"] = sV.inc(i)
        @block.scalar
        def _(e, hh=hh):
            e.wait_ge(sV.h, rec_el["kev"])
            i = e.activation(ELT[:, :], KF[:, :], AF.Relu)
            rec_el["kr"] = sS.inc(i)
        @block.vector
        def _(e, hh=hh):
            e.wait_ge(sS.h, rec_el["kr"])
            i = e.tensor_sub(KF[:, :], KF[:, :], ELT[:, :])
            rec_el["km"] = sV.inc(i)
        @block.scalar
        def _(e, hh=hh):
            e.wait_ge(sV.h, rec_el["km"])
            i = e.activation(KF[:, :], KF[:, :], AF.Exp)
            rec_el["kx"] = sS.inc(i)
        @block.vector
        def _(e, hh=hh):
            e.wait_ge(sS.h, rec_el["kx"])
            i = e.tensor_add(KF[:, :], KF[:, :], ELT[:, :])
            rec_el["kd"] = sV.inc(i)
        @block.sync
        def _(e, hh=hh):
            e.wait_ge(sV.h, rec_el["kev"])
            e.wait_ge(sW.h, sW.n)
            i = e.dma_start(out=WS7[:, :], in_=wv_b[hh])
            rec_w["v"] = sW.inc(i)
        @block.tensor
        def _(e, hh=hh):
            e.wait_ge(sW.h, rec_w["v"])
            for ch in range(NCH):
                for s4 in range(4):
                    i = e.matmul(PS[s4][:, :],
                                 WS7[:, ch * 128:(ch + 1) * 128],
                                 xch(ch)[:, s4 * 512:(s4 + 1) * 512],
                                 start=(ch == 0), stop=(ch == NCH - 1))
                    if ch == NCH - 1:
                        rec_pj[("v", s4)] = sT.inc(i)
        @block.vector
        def _(e, hh=hh):
            for s4 in range(4):
                e.wait_ge(sT.h, rec_pj[("v", s4)])
                i = e.tensor_tensor(VF[:, s4 * 512:(s4 + 1) * 512],
                                    PS[s4][:, :],
                                    BC6[:, s4 * 512:(s4 + 1) * 512], ALU.mult)
            v = sV.inc(i)
            e.wait_ge(sV.h, v)
            i = e.tensor_scalar_add(VF[:, :], VF[:, :], bias_b(2, hh))
            rec_el["vev"] = sV.inc(i)
            rec_el["vb"] = rec_el["vev"]
        rec_tr = {}
        rec_te = {}
        for j in range(NCH):
            @block.tensor
            def _(e, hh=hh, j=j):
                if j == 0:
                    e.wait_ge(sV.h, rec_el["kd"])
                    e.wait_ge(sV.h, rec_el["vb"])
                if j >= 1:
                    e.wait_ge(sV.h, rec_te[j - 1])
                i = e.transpose(PS[4][:, 0:128],
                                KF[:, j * 128:(j + 1) * 128], eye[:, :])
                rec_tr[(j, 0)] = sT.inc(i)
                i = e.transpose(PS[5][:, 0:128],
                                VF[:, j * 128:(j + 1) * 128], eye[:, :])
                rec_tr[(j, 1)] = sT.inc(i)
            @block.vector
            def _(e, hh=hh, j=j):
                e.wait_ge(sT.h, rec_tr[(j, 0)])
                e.tensor_copy(KTL[:, 0:128], PS[4][:, 0:128])
                e.wait_ge(sT.h, rec_tr[(j, 1)])
                i = e.tensor_copy(KTL[:, 128:256], PS[5][:, 0:128])
                rec_te[j] = sV.inc(i)
            @block.tensor
            def _(e, hh=hh, j=j):
                e.wait_ge(sV.h, rec_te[j])
                i = e.matmul(PS[6][:, 0:128], KTL[:, 0:128], KTL[:, 128:256],
                             start=(j == 0), stop=(j == NCH - 1))
                rec_tr[(j, "kv")] = sT.inc(i)
                i = e.matmul(PS[7][:, 0:1], KTL[:, 0:128], ones[:, 0:1],
                             start=(j == 0), stop=(j == NCH - 1))
                rec_tr[(j, "ks")] = sT.inc(i)
                if j == NCH - 1:
                    rec7["t"][hh] = sT.n
        @block.vector
        def _(e, hh=hh):
            e.wait_ge(sT.h, rec_tr[(NCH - 1, "ks")])
            e.tensor_copy(KVB[:, hh * 128:(hh + 1) * 128], PS[6][:, 0:128])
            i = e.tensor_copy(RW[:, RW_KSC + hh:RW_KSC + hh + 1],
                              PS[7][:, 0:1])
            kv_ev[hh] = sV.inc(i)
    @block.gpsimd
    def _(e):
        e.wait_ge(sV.h, kv_ev[H - 1])
        e.wait_ge(sG.h, sG.n)
        sG.inc(e.dma_start(out=cc_i[:, 0:2048], in_=KVB[:, :]))
        e.wait_ge(sG.h, sG.n)
        i = e.dma_start(out=cc_i[:, 2048:2064], in_=RW[:, RW_KSC:RW_KSC + 16])
        sG.inc(i)
        e.wait_ge(sG.h, sG.n)
        i = e.collective_compute(
            "AllReduce", ALU.add, ins=[cc_i[:, :]], outs=[cc_o[:, :]],
            replica_groups=[[0, 1], [2, 3], [4, 5], [6, 7]])
        sC.inc(i)
        e.wait_ge(sC.h, sC.n)
        e.wait_ge(sG.h, sG.n)
        sG.inc(e.dma_start(out=KVB[:, :], in_=cc_o[:, 0:2048]))
        e.wait_ge(sG.h, sG.n)
        i = e.dma_start(out=RW[:, RW_KSC:RW_KSC + 16], in_=cc_o[:, 2048:2064])
        sG.inc(i)
    barrier()

    rec_b = {"t": {-1: sT.n}, "cb": {}}
    for s8 in range(8):
        bcol = slice(s8 * 256, (s8 + 1) * 256)
        m3v = []
        @block.tensor
        def _(e, s8=s8, bcol=bcol):
            e.wait_ge(sV.h, rec_b["cb"].get((s8 - 1, "done"), 0))
            i = e.matmul(PS[6][:, 0:256], ones[0:1, 0:128], rM[:, bcol],
                         start=True, stop=True)
            m3v.append(sT.inc(i))
        @block.vector
        def _(e, s8=s8):
            e.wait_ge(sT.h, m3v[0])
            i = e.tensor_copy(M3B[:, :], PS[6][:, 0:256])
            m3v.append(sV.inc(i))
        for hh in range(H):
            rec_q = {}
            @block.sync
            def _(e, s8=s8, hh=hh):
                e.wait_ge(sT.h, rec_b["t"][(s8, hh - 1)] if hh > 0 else
                          (rec_b["t"][(s8 - 1, "woT", 15)] if s8 > 0
                           else rec_b["t"][-1]))
                e.wait_ge(sW.h, sW.n)
                i = e.dma_start(out=WS7[:, :], in_=wq_b[hh])
                rec_q["w"] = sW.inc(i)
            @block.tensor
            def _(e, s8=s8, hh=hh, bcol=bcol):
                e.wait_ge(sW.h, rec_q["w"])
                if hh >= 2:
                    e.wait_ge(sV.h, rec_b["cb"][(s8, hh - 2)])
                for ch in range(NCH):
                    i = e.matmul(PS[hh % 2][:, 0:256],
                                 WS7[:, ch * 128:(ch + 1) * 128],
                                 xch(ch)[:, bcol], start=(ch == 0),
                                 stop=(ch == NCH - 1))
                rec_q["pj"] = sT.inc(i)
                rec_b["t"][(s8, hh)] = sT.n
            @block.vector
            def _(e, s8=s8, hh=hh, bcol=bcol):
                e.wait_ge(sT.h, rec_q["pj"])
                i = e.tensor_tensor(QE7[:, :], PS[hh % 2][:, 0:256],
                                    BC6[:, bcol], ALU.mult)
                v = sV.inc(i)
                e.wait_ge(sV.h, v)
                i = e.tensor_scalar_add(QE7[:, :], QE7[:, :], bias_b(0, hh))
                rec_q["ev"] = sV.inc(i)
            @block.scalar
            def _(e, s8=s8, hh=hh):
                e.wait_ge(sV.h, rec_q["ev"])
                i = e.activation(QTMP[:, :], QE7[:, :], AF.Relu)
                rec_q["r"] = sS.inc(i)
            @block.vector
            def _(e, s8=s8, hh=hh):
                e.wait_ge(sS.h, rec_q["r"])
                i = e.tensor_sub(QE7[:, :], QE7[:, :], QTMP[:, :])
                rec_q["m"] = sV.inc(i)
            @block.scalar
            def _(e, s8=s8, hh=hh):
                e.wait_ge(sV.h, rec_q["m"])
                i = e.activation(QE7[:, :], QE7[:, :], AF.Exp)
                rec_q["x"] = sS.inc(i)
            @block.vector
            def _(e, s8=s8, hh=hh):
                e.wait_ge(sS.h, rec_q["x"])
                i = e.tensor_add(QE7[:, :], QE7[:, :], QTMP[:, :])
                rec_q["d"] = sV.inc(i)
            @block.tensor
            def _(e, s8=s8, hh=hh):
                e.wait_ge(sV.h, rec_q["d"])
                i = e.matmul(PS[2 + hh % 2][:, 0:256],
                             KVB[:, hh * 128:(hh + 1) * 128], QE7[:, :],
                             start=True, stop=True)
                rec_q["kv"] = sT.inc(i)
                i = e.matmul(PS[4][0:1, 0:256],
                             RW[:, RW_KSC + hh:RW_KSC + hh + 1], QE7[:, :],
                             start=True, stop=True)
                rec_q["ks"] = sT.inc(i)
                rec_b["t"][(s8, hh)] = sT.n
            @block.scalar
            def _(e, s8=s8, hh=hh):
                e.wait_ge(sT.h, rec_q["ks"])
                i = e.activation(rRD[:, 0:256], PS[4][0:1, 0:256], AF.Copy,
                                 bias=1e-8)
                rec_q["eps"] = sS.inc(i)
            @block.vector
            def _(e, s8=s8, hh=hh):
                e.wait_ge(sS.h, rec_q["eps"])
                i = e.reciprocal(rRD[:, 0:256], rRD[:, 0:256])
                rec_q["rc"] = sV.inc(i)
            @block.tensor
            def _(e, s8=s8, hh=hh):
                e.wait_ge(sV.h, rec_q["rc"])
                i = e.matmul(PS[4][:, 0:256], ones[32:33, 0:128],
                             rRD[:, 0:256], start=True, stop=True)
                rec_q["bc"] = sT.inc(i)
                rec_b["t"][(s8, hh)] = sT.n
            @block.vector
            def _(e, s8=s8, hh=hh):
                e.wait_ge(sT.h, rec_q["bc"])
                i = e.tensor_copy(RD7[:, :], PS[4][:, 0:256])
                v = sV.inc(i)
                e.wait_ge(sV.h, v)
                e.wait_ge(sT.h, rec_q["kv"])
                i = e.tensor_tensor(cbch(hh), PS[2 + hh % 2][:, 0:256],
                                    RD7[:, :], ALU.mult)
                rec_b["cb"][(s8, hh)] = sV.inc(i)
        for oc in range(16):
            rec_o = {}
            @block.sync
            def _(e, s8=s8, oc=oc):
                e.wait_ge(sT.h, rec_b["t"][(s8, "woT", oc - 1)]
                          if oc > 0 else rec_b["t"][(s8, H - 1)])
                if oc == 0:
                    e.wait_ge(sV.h, rec_b["cb"][(s8, H - 1)])
                e.wait_ge(sW.h, sW.n)
                i = e.dma_start(out=WS7[:, :], in_=wo_b[oc])
                rec_o["w"] = sW.inc(i)
            @block.tensor
            def _(e, s8=s8, oc=oc):
                e.wait_ge(sW.h, rec_o["w"])
                e.wait_ge(sV.h, rec_b["cb"][(s8, H - 1)])
                if oc >= 2:
                    e.wait_ge(sV.h, rec_b["cb"][(s8, "wo", oc - 2)])
                for ch in range(NCH):
                    i = e.matmul(PS[6 + oc % 2][:, 0:256],
                                 WS7[:, ch * 128:(ch + 1) * 128],
                                 cbch(ch), start=(ch == 0),
                                 stop=(ch == NCH - 1))
                rec_o["pj"] = sT.inc(i)
                rec_b["t"][(s8, "woT", oc)] = sT.n
            @block.vector
            def _(e, s8=s8, oc=oc, bcol=bcol):
                e.wait_ge(sT.h, rec_o["pj"])
                e.wait_ge(sV.h, m3v[1])
                xb = xch(oc)[:, bcol]
                i = e.tensor_tensor(xb, PS[6 + oc % 2][:, 0:256], xb,
                                    ALU.add)
                v = sV.inc(i)
                e.wait_ge(sV.h, v)
                i = e.tensor_add(xb, xb, M3B[:, :])
                v = sV.inc(i)
                e.wait_ge(sV.h, v)
                i = e.tensor_scalar_add(xb, xb, bias_b(3, oc))
                rec_o["ev"] = sV.inc(i)
                rec_b["cb"][(s8, "wo", oc)] = rec_o["ev"]
                if oc == 15:
                    rec_b["cb"][(s8, "done")] = sV.n
    barrier()
    if phases == 3:
        @block.gpsimd
        def _(e):
            for c in range(16):
                e.wait_ge(sG.h, sG.n)
                i = e.dma_start(out=out[c * 128:(c + 1) * 128, :], in_=xch(c))
                sG.inc(i)
            e.wait_ge(sG.h, sG.n)
        blk.__exit__(None, None, None)
        es.close()
        return nc

    stats_center(None, BC6, center=False)
    barrier()
    ffn_phase()
    barrier()

    @block.gpsimd
    def _(e):
        for c in range(16):
            e.wait_ge(sG.h, sG.n)
            i = e.dma_start(out=out[c * 128:(c + 1) * 128, :], in_=xch(c))
            sG.inc(i)
        e.wait_ge(sG.h, sG.n)
    blk.__exit__(None, None, None)
    es.close()
    return nc


def _tile_w(Wt, noc):
    nin = Wt.shape[0]
    r = Wt.reshape(nin // 128, 128, noc, 128).transpose(2, 1, 0, 3)
    return np.ascontiguousarray(r.reshape(noc, 128, nin))


def _rope_perm():
    p = np.zeros(D, dtype=np.int64)
    for hh in range(H):
        p[hh * HD:hh * HD + 64] = hh * HD + np.arange(0, HD, 2)
        p[hh * HD + 64:(hh + 1) * HD] = hh * HD + np.arange(1, HD, 2)
    return p


def prep_inputs(inputs):
    perm = _rope_perm()
    f32 = lambda a: np.ascontiguousarray(np.asarray(a, dtype=np.float32))
    d = {}
    d["wq_a"] = _tile_w(f32(inputs["L0_attn_a_wq"])[perm, :].T.copy(), 16)
    d["wk_a"] = _tile_w(f32(inputs["L0_attn_a_wk"])[perm, :].T.copy(), 16)
    d["wv_a"] = _tile_w(f32(inputs["L0_attn_a_wv"]).T.copy(), 16)
    d["wo_a"] = _tile_w(f32(inputs["L0_attn_a_wo"]).T.copy(), 16)
    d["wq_b"] = _tile_w(f32(inputs["L0_attn_b_wq"]).T.copy(), 16)
    d["wk_b"] = _tile_w(f32(inputs["L0_attn_b_wk"]).T.copy(), 16)
    d["wv_b"] = _tile_w(f32(inputs["L0_attn_b_wv"]).T.copy(), 16)
    d["wo_b"] = _tile_w(f32(inputs["L0_attn_b_wo"]).T.copy(), 16)
    d["bqkvo"] = np.ascontiguousarray(np.concatenate(
        [f32(inputs["L0_attn_b_b" + n]).reshape(16, 128).T
         for n in ("q", "k", "v", "o")], axis=1))
    d["wgu"] = np.stack([_tile_w(f32(inputs["L0_ffn_gate_up_proj"][e]), 32)
                         for e in range(4)])
    d["wdn"] = np.stack([_tile_w(f32(inputs["L0_ffn_down_proj"][e]), 16)
                         for e in range(4)])
    d["invf"] = f32(inputs["L0_attn_a_inv_freq"]).reshape(1, 64)
    d["eyein"] = np.eye(128, dtype=np.float32)
    return d


def core_inputs(shared, xfull, core):
    b, hh = core // 2, core % 2
    own = slice(hh * T, (hh + 1) * T)
    par = slice((1 - hh) * T, (2 - hh) * T)
    xt = np.ascontiguousarray(xfull[b].T)
    xc = np.concatenate([xt[:, own], xt[:, par]], axis=1)
    tv = np.concatenate([np.arange(hh * T, (hh + 1) * T, dtype=np.float32),
                         np.arange((1 - hh) * T, (2 - hh) * T,
                                   dtype=np.float32)]).reshape(1, S)
    d = dict(shared)
    d["xin"] = np.ascontiguousarray(xc)
    d["tvec"] = tv
    return d


def kernel(**inputs):
    nc = build(phases=4)
    shared = prep_inputs(inputs)
    xfull = np.asarray(inputs["L0_attn_a_x"], dtype=np.float32)
    in_maps = [core_inputs(shared, xfull, cidx) for cidx in range(8)]
    res = run_bass_kernel_spmd(nc, in_maps, core_ids=list(range(8)))
    outp = np.empty((B, S, D), dtype=np.float32)
    for cidx in range(8):
        b, hh = cidx // 2, cidx % 2
        outp[b, hh * T:(hh + 1) * T, :] = res.results[cidx]["out"].T
    return outp


if __name__ == "__main__":
    pass
